# version 15
# baseline (speedup 1.0000x reference)
import math
import numpy as np
import ml_dtypes
from contextlib import ExitStack

import concourse.bass as bass
import concourse.mybir as mybir
from concourse.bass_utils import run_bass_kernel_spmd

F32 = mybir.dt.float32
BF16 = mybir.dt.bfloat16
AF = mybir.ActivationFunctionType
ALU = mybir.AluOpType
AX = mybir.AxisListType

D_MODEL = 1024
T = 2048
NCORES = 8


ENGINES = ("tensor", "vector", "scalar", "gpsimd", "sync")
DMA_ENGINES = ("sync", "gpsimd")


class Op:
    __slots__ = ("eng", "pos", "emit", "deps", "is_dma", "marked", "semval")

    def __init__(self, eng, pos, emit, deps, is_dma):
        self.eng = eng
        self.pos = pos
        self.emit = emit
        self.deps = deps
        self.is_dma = is_dma
        self.marked = False
        self.semval = None


class Buf:
    __slots__ = ("name", "w", "rs")

    def __init__(self, name):
        self.name = name
        self.w = None
        self.rs = []


class Prog:
    def __init__(self):
        self.ops = {e: [] for e in ENGINES}
        self.pending = {e: set() for e in ENGINES}
        self.last_dma = {}

    def add(self, eng, emit, reads=(), writes=(), deps=(), dma=False):
        d = set(x for x in deps if x is not None)
        if dma:
            prev = self.last_dma.get(eng)
            if prev is not None:
                d.add(prev)
        for b in reads:
            if b.w is not None:
                d.add(b.w)
        for b in writes:
            if b.w is not None:
                d.add(b.w)
            d.update(b.rs)
        d.update(self.pending[eng])
        self.pending[eng] = set()
        op = Op(eng, len(self.ops[eng]), emit, d, dma)
        op.deps.discard(op)
        self.ops[eng].append(op)
        if dma:
            self.last_dma[eng] = op
        for b in reads:
            b.rs.append(op)
        for b in writes:
            b.w = op
            b.rs = []
        return op

    def barrier(self):
        last = [self.ops[e][-1] for e in ENGINES if self.ops[e]]
        for e in ENGINES:
            self.pending[e].update(last)

    def emit_all(self, nc, block, sems, final_waits=()):
        for e in ENGINES:
            for op in self.ops[e]:
                if op.is_dma:
                    op.marked = True
                for dep in op.deps:
                    if dep.eng != op.eng or dep.is_dma:
                        dep.marked = True
        for op in final_waits:
            op.marked = True
        def semkey(op):
            return "gdma" if (op.is_dma and op.eng == "gpsimd") else op.eng

        cnt = {e: 0 for e in list(ENGINES) + ["gdma"]}
        for e in ENGINES:
            for op in self.ops[e]:
                if op.marked:
                    k = semkey(op)
                    cnt[k] += 16 if op.is_dma else 1
                    op.semval = cnt[k]
        def emit_engine(engine_handle, eng):
            watermark = {e: 0 for e in list(ENGINES) + ["gdma"]}
            for op in self.ops[eng]:
                need = {}
                for dep in op.deps:
                    if dep.eng == eng and not dep.is_dma:
                        continue
                    if dep.semval is None:
                        raise RuntimeError("unmarked dep")
                    k = semkey(dep)
                    if dep.semval > need.get(k, 0):
                        need[k] = dep.semval
                for se, v in sorted(need.items()):
                    if v > watermark[se]:
                        engine_handle.wait_ge(sems[se], v)
                        watermark[se] = v
                inst = op.emit(engine_handle)
                if op.marked:
                    inst.then_inc(sems[semkey(op)], 16 if op.is_dma else 1)
            if eng == "sync":
                for op in final_waits:
                    if op.semval is not None:
                        engine_handle.wait_ge(sems[semkey(op)], op.semval)

        @block.tensor
        def _(t):
            emit_engine(t, "tensor")

        @block.vector
        def _(v):
            emit_engine(v, "vector")

        @block.scalar
        def _(s):
            emit_engine(s, "scalar")

        @block.gpsimd
        def _(g):
            emit_engine(g, "gpsimd")

        @block.sync
        def _(s):
            emit_engine(s, "sync")



def _rotary_tables():
    DH, WINDOW, n = 64, 512, T
    half = np.arange(0, DH, 2, dtype=np.float64)
    inv_freq = 1.0 / 10000.0 ** (half / DH)
    t = np.arange(n, dtype=np.float64)
    freqs = t[:, None] * inv_freq[None, :]
    freqs = np.concatenate([freqs, freqs], axis=-1)
    base = (half + 0.4 * DH) / (1.4 * DH)
    power = (t - n // 2) / float(WINDOW // 2)
    scale = base[None, :] ** power[:, None]
    scale = np.concatenate([scale, scale], axis=-1)
    cos, sin = np.cos(freqs), np.sin(freqs)
    QC = (cos * scale).astype(np.float32)
    QS = (sin * scale).astype(np.float32)
    KC = (cos / scale).astype(np.float32)
    KS = (sin / scale).astype(np.float32)
    tile8 = lambda a: np.ascontiguousarray(np.tile(a, (1, 8)))
    return tile8(QC), tile8(QS), tile8(KC), tile8(KS)


def _dft_mats():
    D = D_MODEL
    k = np.arange(D, dtype=np.float64)
    ang = 2.0 * np.pi * np.outer(k, k) / D
    mats = {
        "cr": np.cos(ang), "cin": np.sin(ang),
        "gr": np.cos(ang) / D, "gi": np.sin(ang) / D, "grn": -np.cos(ang) / D,
    }
    out = {}
    for n, m in mats.items():
        mf = m.astype(np.float32)
        hi = mf.astype(ml_dtypes.bfloat16)
        lo = (mf - hi.astype(np.float32)).astype(ml_dtypes.bfloat16)
        out[n + "_h"] = np.ascontiguousarray(hi)
        out[n + "_l"] = np.ascontiguousarray(lo)
    return out


def _col_layout(v, pad_to=None):
    v = np.asarray(v, dtype=np.float32).reshape(-1)
    if pad_to is not None and v.size < pad_to:
        v = np.concatenate([v, np.zeros(pad_to - v.size, np.float32)])
    n = v.size
    assert n % 128 == 0
    return np.ascontiguousarray(v.reshape(n // 128, 128).T)


def _l1_aug_rows():
    q = np.arange(T, dtype=np.float32)
    qrel = q % 512 - 256
    cqv = 512 * (q // 512) + 256
    QA0 = np.stack([qrel, -np.ones(T, np.float32), cqv, -np.ones(T, np.float32)])
    k = np.arange(T, dtype=np.float32)
    krel = k % 128 - 64
    ckv = 128 * (k // 128) + 64
    Kaug = np.stack([np.ones(T, np.float32), krel, np.ones(T, np.float32), ckv])
    return QA0.astype(np.float32), np.ascontiguousarray(Kaug.astype(ml_dtypes.bfloat16))


def _l1_diag_tile():
    i = np.arange(128)
    d = -np.abs(i[:, None] - i[None, :]).astype(np.float32)
    return np.ascontiguousarray(d)


def _l0ffn_masks():
    i = np.arange(128)
    m = np.where(i[:, None] <= i[None, :], 0.0, -1e30).astype(np.float32)
    return np.ascontiguousarray(m)


def _prep_host(inputs):
    f32 = np.float32
    ctg = np.ascontiguousarray
    x = np.asarray(inputs["L0_attn_x"], f32)

    shared = {}
    shared["w_inT"] = ctg(np.asarray(inputs["L0_attn_W_in"], f32).T)
    shared["w_xT"] = ctg(np.asarray(inputs["L0_attn_W_x"], f32).T)
    shared["w_dtT"] = ctg(np.asarray(inputs["L0_attn_W_dt"], f32).T)
    shared["w_outT"] = ctg(np.asarray(inputs["L0_attn_W_out"], f32).T)
    cw = np.asarray(inputs["L0_attn_conv_w"], f32).reshape(2048, 4)
    shared["conv_w"] = ctg(cw.reshape(16, 128, 4).transpose(1, 0, 2).reshape(128, 64))
    shared["conv_b"] = _col_layout(inputs["L0_attn_conv_b"])
    shared["b_dt"] = _col_layout(inputs["L0_attn_b_dt"])
    al = np.asarray(inputs["L0_attn_A_log"], f32)
    shared["a_log"] = ctg(al.reshape(16, 128, 16).transpose(1, 0, 2).reshape(128, 256))
    shared["d_vec"] = _col_layout(inputs["L0_attn_D"])
    shared["ln_g0"] = _col_layout(inputs["L0_attn_ln_g"])
    shared["ln_b0"] = _col_layout(inputs["L0_attn_ln_b"])

    wqkv = np.asarray(inputs["L0_ffn_wqkv"], f32)
    shared["wqkv"] = ctg(wqkv)
    shared["wo_f"] = ctg(np.asarray(inputs["L0_ffn_wo"], f32))
    shared["w1"] = ctg(np.asarray(inputs["L0_ffn_w1"], f32))
    shared["w2"] = ctg(np.asarray(inputs["L0_ffn_w2"], f32))
    b1 = np.asarray(inputs["L0_ffn_b1"], f32)
    shared["b1a"] = np.stack([_col_layout(b1[l, :2730], pad_to=22 * 128) for l in range(2)])
    shared["b1g"] = np.stack([_col_layout(b1[l, 2730:], pad_to=22 * 128) for l in range(2)])
    shared["b2"] = np.stack([_col_layout(np.asarray(inputs["L0_ffn_b2"], f32)[l]) for l in range(2)])
    shared["lng_f"] = np.stack([_col_layout(np.asarray(inputs["L0_ffn_ln_g"], f32)[l]) for l in range(2)])
    shared["lnb_f"] = np.stack([_col_layout(np.asarray(inputs["L0_ffn_ln_b"], f32)[l]) for l in range(2)])
    qs = np.asarray(inputs["L0_ffn_q_scale"], f32)
    ks = np.asarray(inputs["L0_ffn_k_scale"], f32)
    shared["qs_row"] = ctg(np.tile(qs, (1, 8)))
    shared["ks_row"] = ctg(np.tile(ks, (1, 8)))

    shared["wqT"] = ctg(np.asarray(inputs["L1_attn_Wq"], f32).T)
    shared["wkT"] = ctg(np.asarray(inputs["L1_attn_Wk"], f32).T)
    shared["wvT"] = ctg(np.asarray(inputs["L1_attn_Wv"], f32).T)
    shared["woT"] = ctg(np.asarray(inputs["L1_attn_Wo"], f32).T)
    shared["bo"] = _col_layout(inputs["L1_attn_bo"])
    shared["l1fb"] = _col_layout(inputs["L1_ffn_bias"])
    shared["wrT"] = ctg(np.asarray(inputs["L1_ffn_w_freq_real"], f32).T)
    shared["wiT"] = ctg(np.asarray(inputs["L1_ffn_w_freq_imag"], f32).T)

    QC, QS, KC, KS = _rotary_tables()
    shared["rot_qc"], shared["rot_qs"] = QC, QS
    shared["rot_kc"], shared["rot_ks"] = KC, KS
    for nk, mv in _dft_mats().items():
        shared["dft_" + nk] = mv
    QA0, Kaug = _l1_aug_rows()
    shared["qa0"], shared["kaug"] = QA0, Kaug
    shared["dtile"] = _l1_diag_tile()
    shared["ffn_mask"] = _l0ffn_masks()
    sel = np.zeros((32, 32 * 128), np.float32)
    for n in range(32):
        sel[n, n * 128:(n + 1) * 128] = 1.0
    shared["sel32"] = ctg(sel)

    in_maps = []
    for c in range(NCORES):
        m = dict(shared)
        m["x_in"] = ctg(x[c].T)
        in_maps.append(m)
    return in_maps



class MK:

    def __init__(self, nc, es):
        self.nc = nc
        self.es = es
        self.p = Prog()
        self.bufs = {}
        self.t = {}
        self.out_store_ops = []

    def sb(self, name, shape, dtype, es=None):
        h = (es or self.es).enter_context(self.nc.sbuf_tensor(name, shape, dtype, allow_name_mangling=True))
        self.t[name] = h
        self.bufs[name] = Buf(name)
        return h

    def dram(self, name, shape, dtype, kind=None):
        if kind is None:
            h = self.nc.dram_tensor(name, shape, dtype)
        else:
            h = self.nc.dram_tensor(name, shape, dtype, kind=kind)
        self.t[name] = h
        self.bufs[name] = Buf(name)
        return h

    def B(self, name):
        return self.bufs[name]

    def op(self, eng, emit, r=(), w=(), dma=False):
        rb = [self.bufs[x] if isinstance(x, str) else x for x in r]
        wb = [self.bufs[x] if isinstance(x, str) else x for x in w]
        return self.p.add(eng, emit, reads=rb, writes=wb, dma=dma)

    def load(self, dst_ap, src_ap, r=(), w=(), q="sync"):
        return self.op(q, (lambda e, d=dst_ap, s=src_ap: e.dma_start(out=d, in_=s)),
                       r=r, w=w, dma=True)

    def store(self, dst_ap, src_ap, r=(), w=(), q="gpsimd", slow=False):
        return self.op(q, (lambda e, d=dst_ap, s=src_ap, sl=slow:
                           e.dma_start(out=d, in_=s, allow_slow_non_contiguous=sl)),
                       r=r, w=w, dma=True)

    def mm(self, out_ap, lhsT_ap, rhs_ap, start, stop, r=(), w=()):
        return self.op("tensor",
                       (lambda e, o=out_ap, l=lhsT_ap, rr=rhs_ap, st=start, sp=stop:
                        e.matmul(o, l, rr, start=st, stop=sp)), r=r, w=w)

    def act(self, out_ap, in_ap, func, bias=0.0, scale=1.0, accum=None, r=(), w=()):
        def f(e, o=out_ap, i=in_ap, fn=func, b=bias, s=scale, a=accum):
            return e.activation(out=o, in_=i, func=fn, bias=b, scale=s, accum_out=a)
        return self.op("scalar", f, r=r, w=w)

    def vtt(self, out_ap, a_ap, b_ap, alu, r=(), w=()):
        return self.op("vector",
                       (lambda e, o=out_ap, a=a_ap, b=b_ap, op2=alu:
                        e.tensor_tensor(out=o, in0=a, in1=b, op=op2)), r=r, w=w)

    def vstt(self, out_ap, in0_ap, scalar, in1_ap, op0, op1, r=(), w=()):
        return self.op("vector",
                       (lambda e, o=out_ap, i0=in0_ap, sc=scalar, i1=in1_ap, a0=op0, a1=op1:
                        e.scalar_tensor_tensor(out=o, in0=i0, scalar=sc, in1=i1,
                                               op0=a0, op1=a1)), r=r, w=w)

    def vts(self, out_ap, in0_ap, s1, s2, op0, op1=None, r=(), w=()):
        def f(e, o=out_ap, i=in0_ap, a=s1, b=s2, o0=op0, o1=op1):
            if b is None:
                return e.tensor_single_scalar(out=o, in_=i, scalar=a, op=o0)
            return e.tensor_scalar(out=o, in0=i, scalar1=a, scalar2=b, op0=o0, op1=o1)
        return self.op("vector", f, r=r, w=w)

    def vcopy(self, out_ap, in_ap, r=(), w=(), eng="vector"):
        if eng == "scalar":
            return self.op(eng, (lambda e, o=out_ap, i=in_ap: e.copy(out=o, in_=i)),
                           r=r, w=w)
        return self.op(eng, (lambda e, o=out_ap, i=in_ap: e.tensor_copy(out=o, in_=i)),
                       r=r, w=w)

    def vscan(self, out_ap, d0, d1, init, r=(), w=()):
        return self.op("vector",
                       (lambda e, o=out_ap, a=d0, b=d1, ini=init:
                        e.tensor_tensor_scan(out=o, data0=a, data1=b, initial=ini,
                                             op0=ALU.mult, op1=ALU.add)), r=r, w=w)

    def alloc_persistent(self):
        sb = self.sb
        sb("bfbuf", [128, 8 * 2049], BF16)
        sb("wf0", [128, 2816], F32)
        sb("wb0", [128, 2816], BF16); sb("wb1", [128, 2816], BF16)
        sb("ones_bf", [128, 128], BF16)
        sb("ones_f", [128, 128], F32)
        sb("id_f", [128, 128], F32)
        sb("id_b", [128, 128], BF16)
        sb("csts", [128, 8], F32)
        self._cc = {}
        sb("mrow", [128, 512], F32)
        sb("vrow", [128, 512], F32)
        sb("tmp_s", [128, 512], F32)
        sb("xch", [128, 2 * 512], F32)
        sb("stg", [128, 2048], F32)
        sb("stgb", [128, 2048], BF16)
        self.t["psA"] = self.es.enter_context(self.nc.psum_tensor("psA", [128, 2048], F32))
        self.t["psB"] = self.es.enter_context(self.nc.psum_tensor("psB", [128, 1536], F32))
        self.t["psT"] = self.es.enter_context(self.nc.psum_tensor("psT", [128, 1024], BF16))
        for k in range(8):
            self.bufs[f"PS{k}"] = Buf(f"PS{k}")
        for s in range(2):
            self.bufs[f"lnxb{s}"] = Buf(f"lnxb{s}")
            self.bufs[f"lnx2{s}"] = Buf(f"lnx2{s}")

    def dbg(self, name, ap, dt_, rb=()):
        if name not in _DBG:
            return
        sh = list(ap.shape)
        o = self.dram("dbg_" + name, sh, dt_, kind="ExternalOutput").ap()
        st = self.store(o, ap, r=rb, q="gpsimd")
        self.out_store_ops.append(st)

    def ccol(self, v):
        if v not in self._cc:
            i = len(self._cc)
            self._cc[v] = i
            col = self.t["csts"][:, i:i + 1]
            self.op("gpsimd", (lambda e, c=col, vv=v: e.memset(c, vv)), w=["csts"])
        i = self._cc[v]
        return self.t["csts"][:, i:i + 1]

    def psum(self, k, w=512):
        if k < 4:
            ap = self.t["psA"][:, k * 512:k * 512 + w]
        elif k < 7:
            ap = self.t["psB"][:, (k - 4) * 512:(k - 4) * 512 + w]
        else:
            ap = self.t["psT"][:, 0:w]
        return ap, f"PS{k}"

    def init_consts(self):
        nc = self.nc
        for name in ("id_f", "id_b"):
            t = self.t[name]
            self.op("gpsimd", lambda e, tt=t: e.memset(tt[:], 1.0), w=[name])
            self.op("gpsimd",
                    (lambda e, tt=t: e.affine_select(out=tt[:], in_=tt[:],
                     compare_op=ALU.is_equal, fill=0.0, base=0,
                     pattern=[[-1, 128]], channel_multiplier=1)), r=[name], w=[name])
        self.op("gpsimd", lambda e: e.memset(self.t["ones_bf"][:], 1.0), w=["ones_bf"])
        self.op("gpsimd", lambda e: e.memset(self.t["ones_f"][:], 1.0), w=["ones_f"])

    def declare_io(self):
        d = self.dram
        inp = {}
        for name, sh in [
            ("x_in", [1024, 2048]), ("w_inT", [1024, 4096]), ("w_xT", [2048, 2080]),
            ("w_dtT", [2048, 2048]), ("w_outT", [2048, 1024]), ("conv_w", [128, 64]),
            ("conv_b", [128, 16]), ("b_dt", [128, 16]), ("a_log", [128, 256]),
            ("d_vec", [128, 16]), ("ln_g0", [128, 8]), ("ln_b0", [128, 8]),
            ("wqkv", [2, 1024, 1536]), ("wo_f", [2, 512, 1024]),
            ("w1", [2, 1024, 5460]), ("w2", [2, 2730, 1024]),
            ("b1a", [2, 128, 22]), ("b1g", [2, 128, 22]), ("b2", [2, 128, 8]),
            ("lng_f", [2, 128, 8]), ("lnb_f", [2, 128, 8]),
            ("qs_row", [2, 512]), ("ks_row", [2, 512]),
            ("wqT", [1024, 1024]), ("wkT", [1024, 512]), ("wvT", [1024, 512]),
            ("woT", [1024, 1024]), ("bo", [128, 8]), ("l1fb", [128, 8]),
            ("wrT", [1024, 1024]), ("wiT", [1024, 1024]),
            ("rot_qc", [2048, 512]), ("rot_qs", [2048, 512]),
            ("rot_kc", [2048, 512]), ("rot_ks", [2048, 512]),
            ("qa0", [4, 2048]), ("dtile", [128, 128]), ("ffn_mask", [128, 128]),
            ("sel32", [32, 4096]),
        ]:
            inp[name] = d("d_" + name, sh, F32, kind="ExternalInput").ap()
        dftnames = ["dft_%s_%s" % (a, b) for a in ("cr", "cin", "gr", "gi", "grn")
                    for b in ("h", "l")]
        for name, sh in [(n, [1024, 1024]) for n in dftnames] + [("kaug", [4, 2048])]:
            inp[name] = d("d_" + name, sh, BF16, kind="ExternalInput").ap()
        self.inp = inp
        self.out_y = d("y", [1024, 2048], F32, kind="ExternalOutput").ap()
        self.sc_xacc = d("sc_xacc", [1024, 2048], F32).ap()
        self.sc_u = d("sc_u", [2048, 2048], BF16).ap()
        self.sc_zs = d("sc_zs", [2048, 2048], BF16).ap()
        self.x_v = inp["x_in"].rearrange("(c p) t -> p c t", p=128)
        self.xacc_v = self.sc_xacc.rearrange("(c p) t -> p c t", p=128)
        self.u_v = self.sc_u.rearrange("(c p) t -> p c t", p=128)
        self.zs_v = self.sc_zs.rearrange("(c p) t -> p c t", p=128)
        self.y_v = self.out_y.rearrange("(c p) t -> p c t", p=128)

    def emit_ln(self, tc, get_x, post, n_dc=8, denom=1024.0):
        nc = self.nc
        pS, bS = self.psum(4, tc)
        pQ, bQ = self.psum(5, tc)
        ones = self.t["ones_bf"]
        for dc in range(n_dc):
            xap, rb = get_x(dc)
            slot = dc % 2
            xb = self.t["stgb"][:, slot * 512: slot * 512 + tc]
            x2 = self.t["stgb"][:, 1024 + slot * 512: 1024 + slot * 512 + tc]
            bxb = self.bufs[f"lnxb{slot}"]
            bx2 = self.bufs[f"lnx2{slot}"]
            self.vcopy(xb, xap, r=rb, w=[bxb], eng="vector")
            self.act(x2, xap, AF.Square, r=rb, w=[bx2])
            self.mm(pS, ones[:], xb, dc == 0, dc == n_dc - 1, r=["ones_bf", bxb], w=[bS])
            self.mm(pQ, ones[:], x2, dc == 0, dc == n_dc - 1, r=["ones_bf", bx2], w=[bQ])
        m = self.t["mrow"][:, 0:tc]
        v = self.t["vrow"][:, 0:tc]
        ts = self.t["tmp_s"][:, 0:tc]
        self.vts(m, pS, 1.0 / denom, None, ALU.mult, r=[bS], w=["mrow"])
        self.act(ts, m, AF.Square, r=["mrow"], w=["tmp_s"])
        self.vstt(v, pQ, 1.0 / denom, ts, ALU.mult, ALU.subtract,
                  r=[bQ, "tmp_s"], w=["vrow"])
        self.act(v, v, AF.Ln, bias=self.ccol(1e-5), r=["vrow", "csts"], w=["vrow"])
        self.act(v, v, AF.Exp, scale=-0.5, r=["vrow"], w=["vrow"])
        for dc in range(n_dc):
            post(dc, m, v)

    def loadW(self, src_3d, col0, ncols, nkc, slot, rows32=False):
        wf = self.t["wf0"]; wb = self.t[f"wb{slot}"]
        bf_ = "wf0"; bb = f"wb{slot}"
        tot = nkc * ncols
        dst = wf.ap()[:, 0:tot].rearrange("p (kc c) -> p kc c", c=ncols)
        src = src_3d[:, 0:nkc, col0:col0 + ncols]
        self.load(dst, src, w=[bf_])
        self.vcopy(wb[:, 0:tot], wf[:, 0:tot], r=[bf_], w=[bb], eng="scalar")
        return wb, bb

    def mamba(self):
        sb = self.sb
        sb("aall", [128, 256], F32)
        sb("cvw", [128, 64], F32)
        for n in ("cvb", "bdt", "dvec"):
            sb(n, [128, 16], F32)
        sb("lng0", [128, 8], F32)
        sb("lnb0", [128, 8], F32)
        sb("sel32b", [32, 4096], BF16)
        stg, stgb = self.t["stg"], self.t["stgb"]

        for nm, src in [("cvw", "conv_w"), ("cvb", "conv_b"), ("bdt", "b_dt"),
                        ("dvec", "d_vec"), ("lng0", "ln_g0"), ("lnb0", "ln_b0")]:
            self.load(self.t[nm][:], self.inp[src], w=[nm])
        self.load(stg[:, 0:256], self.inp["a_log"], w=["stg"])
        self.act(self.t["aall"][:], stg[:, 0:256], AF.Exp, r=["stg"], w=["aall"])
        self.vts(self.t["aall"][:], self.t["aall"][:], -1.0, None, ALU.mult,
                 r=["aall"], w=["aall"])
        self.load(stg[0:32, 0:2048], self.inp["sel32"][:, 0:2048], w=["stg"])
        self.vcopy(self.t["sel32b"][0:32, 0:2048], stg[0:32, 0:2048],
                   r=["stg"], w=["sel32b"], eng="gpsimd")
        self.load(stg[0:32, 0:2048], self.inp["sel32"][:, 2048:4096], w=["stg"])
        self.vcopy(self.t["sel32b"][0:32, 2048:4096], stg[0:32, 0:2048],
                   r=["stg"], w=["sel32b"], eng="gpsimd")
        self.op("gpsimd", lambda e: e.memset(self.t["bfbuf"][:], 0.0), w=["bfbuf"])

        def load_x(dc, t0, slot):
            xap = self.t["xch"][:, slot * 512:slot * 512 + 512]
            self.load(xap, self.x_v[:, dc, t0:t0 + 512], w=["xch"])
            return xap

        bfb = self.t["bfbuf"]
        for ck in range(4):
            t0 = ck * 512

            def getx(dc, t0=t0):
                return load_x(dc, t0, dc % 2), ["xch"]

            def post(dc, m, r, t0=t0):
                x_ap = load_x(dc, t0, dc % 2)
                t1 = stg[:, 0:512]
                self.vtt(t1, x_ap, m, ALU.subtract, r=["xch", "mrow"], w=["stg"])
                dst_bf = bfb[:, dc * 2049 + 1 + t0: dc * 2049 + 1 + t0 + 512]
                self.vtt(dst_bf, t1, r, ALU.mult, r=["stg", "vrow"], w=["bfbuf"])

            self.emit_ln(512, getx, post)

        winv = self.inp["w_inT"].rearrange("(kc p) m -> p kc m", p=128)
        with ExitStack() as es_a:
            xs = es_a.enter_context(self.nc.sbuf_tensor("xs", [128, 2051], F32))
            ubstg = es_a.enter_context(self.nc.sbuf_tensor("ubstg", [128, 2048], BF16))
            self.bufs["xs"] = Buf("xs"); self.bufs["ubstg"] = Buf("ubstg")
            self.op("gpsimd", lambda e: e.memset(xs[:, 0:3], 0.0), w=["xs"])
            for ot in range(32):
                wb, bb = self.loadW(winv, ot * 128, 128, 8, ot % 2)
                pm, bpm = self.psum(ot % 2, 512)
                is_z = ot >= 16
                for tc4 in range(4):
                    for kc in range(8):
                        rhs = bfb[:, kc * 2049 + 1 + tc4 * 512:
                                  kc * 2049 + 1 + tc4 * 512 + 512]
                        self.mm(pm, wb[:, kc * 128:(kc + 1) * 128], rhs,
                                kc == 0, kc == 7, r=[bb, "bfbuf"], w=[bpm])
                    if is_z:
                        self.act(ubstg[:, tc4 * 512:tc4 * 512 + 512], pm, AF.Silu,
                                 r=[bpm], w=["ubstg"])
                    else:
                        self.act(xs[:, 3 + tc4 * 512: 3 + tc4 * 512 + 512], pm,
                                 AF.Copy, r=[bpm], w=["xs"])
                if is_z:
                    self.store(self.zs_v[:, ot - 16, :], ubstg[:, 0:2048], r=["ubstg"])
                else:
                    cw = self.t["cvw"]
                    self.vts(stg[:, 0:2048], xs[:, 0:2048], cw[:, ot * 4:ot * 4 + 1],
                             None, ALU.mult, r=["xs", "cvw"], w=["stg"])
                    for k in (1, 2, 3):
                        self.vstt(stg[:, 0:2048], xs[:, k:2048 + k],
                                  cw[:, ot * 4 + k:ot * 4 + k + 1], stg[:, 0:2048],
                                  ALU.mult, ALU.add, r=["xs", "cvw", "stg"], w=["stg"])
                    self.act(ubstg[:, 0:2048], stg[:, 0:2048], AF.Silu,
                             bias=self.t["cvb"][:, ot:ot + 1],
                             r=["stg", "cvb"], w=["ubstg"])
                    self.store(self.u_v[:, ot, :], ubstg[:, 0:2048], r=["ubstg"])
            self.p.barrier()

        es_bc = ExitStack()
        for nm, shp, dt_ in [("utc", [128, 16 * 512], BF16), ("zsdt", [128, 512], BF16),
                             ("dintc", [128, 16 * 512], BF16), ("bcbf", [32, 512], BF16),
                             ("brep", [128, 16 * 512], BF16), ("crep", [128, 16 * 512], BF16),
                             ("dabuf", [128, 512], BF16), ("dbubuf", [128, 512], BF16),
                             ("s_all", [128, 16 * 512], BF16), ("ybftc", [128, 16 * 512], BF16),
                             ("sinit", [128, 256], F32), ("xnf", [128, 8 * 512], F32)]:
            self.sb(nm, shp, dt_, es=es_bc)
        self.op("gpsimd", lambda e: e.memset(self.t["sinit"][:], 0.0), w=["sinit"])
        wxv = self.inp["w_xT"].rearrange("(kc p) m -> p kc m", p=128)
        wdv = self.inp["w_dtT"].rearrange("(kc p) m -> p kc m", p=128)
        wov = self.inp["w_outT"].rearrange("(kc p) m -> p kc m", p=128)
        uv3 = self.sc_u.rearrange("(kc p) t -> p kc t", p=128)
        zv3 = self.sc_zs.rearrange("(kc p) t -> p kc t", p=128)
        utc, dintc = self.t["utc"], self.t["dintc"]
        for tcc in range(4):
            t0 = tcc * 512
            self.load(utc.ap().rearrange("p (kc t) -> p kc t", t=512),
                      uv3[:, :, t0:t0 + 512], w=["utc"])
            for ot in range(17):
                ncols = 128 if ot < 16 else 32
                wb, bb = self.loadW(wxv, ot * 128, ncols, 16, ot % 2)
                pm, bpm = self.psum(ot % 2, 512)
                out_ap = pm if ot < 16 else pm[0:32, :]
                for kc in range(16):
                    self.mm(out_ap, wb[:, kc * ncols:(kc + 1) * ncols],
                            utc[:, kc * 512:(kc + 1) * 512],
                            kc == 0, kc == 15, r=[bb, "utc"], w=[bpm])
                if ot < 16:
                    self.act(dintc[:, ot * 512:(ot + 1) * 512], pm, AF.Copy,
                             r=[bpm], w=["dintc"])
                else:
                    self.act(self.t["bcbf"][0:32, :], pm[0:32, :], AF.Copy,
                             r=[bpm], w=["bcbf"])
            selb = self.t["sel32b"]
            for n in range(32):
                pm, bpm = self.psum(2 + (n % 2), 512)
                self.mm(pm, selb[0:32, n * 128:(n + 1) * 128], self.t["bcbf"][0:32, :],
                        True, True, r=["sel32b", "bcbf"], w=[bpm])
                dst = self.t["brep"] if n < 16 else self.t["crep"]
                dn = "brep" if n < 16 else "crep"
                self.act(dst[:, (n % 16) * 512:((n % 16) + 1) * 512], pm, AF.Copy,
                         r=[bpm], w=[dn])
            for dt in range(16):
                wb, bb = self.loadW(wdv, dt * 128, 128, 16, dt % 2)
                pm, bpm = self.psum(dt % 2, 512)
                for kc in range(16):
                    self.mm(pm, wb[:, kc * 128:(kc + 1) * 128],
                            dintc[:, kc * 512:(kc + 1) * 512],
                            kc == 0, kc == 15, r=[bb, "dintc"], w=[bpm])
                self.act(stg[:, 0:512], pm, AF.Exp, bias=self.t["bdt"][:, dt:dt + 1],
                         r=[bpm, "bdt"], w=["stg"])
                self.act(stg[:, 512:1024], stg[:, 0:512], AF.Ln, bias=1.0,
                         r=["stg"], w=["stg"])
                self.vcopy(stgb[:, 0:512], stg[:, 512:1024], r=["stg"], w=["stgb"])
                self.vtt(stgb[:, 512:1024], stgb[:, 0:512],
                         utc[:, dt * 512:(dt + 1) * 512], ALU.mult,
                         r=["stgb", "utc"], w=["stgb"])
                for n in range(16):
                    da, db = self.t["dabuf"][:], self.t["dbubuf"][:]
                    self.act(da, stg[:, 512:1024], AF.Exp,
                             scale=self.t["aall"][:, dt * 16 + n:dt * 16 + n + 1],
                             r=["stg", "aall"], w=["dabuf"])
                    self.vtt(db, stgb[:, 512:1024],
                             self.t["brep"][:, n * 512:(n + 1) * 512], ALU.mult,
                             r=["stgb", "brep"], w=["dbubuf"])
                    self.vscan(self.t["s_all"][:, n * 512:(n + 1) * 512], da, db,
                               self.t["sinit"][:, dt * 16 + n:dt * 16 + n + 1],
                               r=["dabuf", "dbubuf", "sinit"], w=["s_all"])
                sview = self.t["s_all"].ap().rearrange("p (n t) -> p n t", t=512)
                self.vcopy(self.t["sinit"][:, dt * 16:dt * 16 + 16].unsqueeze(2),
                           sview[:, :, 511:512], r=["s_all"], w=["sinit"],
                           eng="vector")
                sa = self.t["s_all"]
                self.vtt(sa[:, 0:8192], sa[:, 0:8192], self.t["crep"][:, 0:8192],
                         ALU.mult, r=["s_all", "crep"], w=["s_all"])
                for width in (4096, 2048, 1024, 512):
                    self.vtt(sa[:, 0:width], sa[:, 0:width], sa[:, width:2 * width],
                             ALU.add, r=["s_all"], w=["s_all"])
                self.load(self.t["zsdt"][:], zv3[:, dt, t0:t0 + 512], w=["zsdt"])
                self.vstt(stgb[:, 1024:1536], utc[:, dt * 512:(dt + 1) * 512],
                          self.t["dvec"][:, dt:dt + 1], sa[:, 0:512], ALU.mult, ALU.add,
                          r=["utc", "dvec", "s_all"], w=["stgb"])
                self.vtt(self.t["ybftc"][:, dt * 512:(dt + 1) * 512], stgb[:, 1024:1536],
                         self.t["zsdt"][:], ALU.mult, r=["stgb", "zsdt"], w=["ybftc"])
            def getx(dc, t0=t0):
                return load_x(dc, t0, dc % 2), ["xch"]

            def post_xn(dc, m, r, t0=t0):
                x_ap = load_x(dc, t0, dc % 2)
                t1 = stg[:, 1024:1536]
                self.vtt(t1, x_ap, m, ALU.subtract, r=["xch", "mrow"], w=["stg"])
                self.vtt(self.t["xnf"][:, dc * 512:(dc + 1) * 512], t1, r, ALU.mult,
                         r=["stg", "vrow"], w=["xnf"])

            self.emit_ln(512, getx, post_xn)
            for ot in range(8):
                wb, bb = self.loadW(wov, ot * 128, 128, 16, ot % 2)
                pm, bpm = self.psum(ot % 2, 512)
                for kc in range(16):
                    self.mm(pm, wb[:, kc * 128:(kc + 1) * 128],
                            self.t["ybftc"][:, kc * 512:(kc + 1) * 512],
                            kc == 0, kc == 15, r=[bb, "ybftc"], w=[bpm])
                self.vtt(self.t["xnf"][:, ot * 512:(ot + 1) * 512], pm,
                         self.t["xnf"][:, ot * 512:(ot + 1) * 512], ALU.add,
                         r=[bpm, "xnf"], w=["xnf"])
            def getmo(dc):
                return self.t["xnf"][:, dc * 512:(dc + 1) * 512], ["xnf"]

            def post_mo(dc, m, r, t0=t0):
                mo = self.t["xnf"][:, dc * 512:(dc + 1) * 512]
                x_ap = load_x(dc, t0, dc % 2)
                t1 = stg[:, 1024:1536]
                self.vtt(t1, mo, m, ALU.subtract, r=["xnf", "mrow"], w=["stg"])
                self.vtt(t1, t1, r, ALU.mult, r=["stg", "vrow"], w=["stg"])
                self.vts(t1, t1, self.t["lng0"][:, dc:dc + 1],
                         self.t["lnb0"][:, dc:dc + 1], ALU.mult, ALU.add,
                         r=["stg", "lng0", "lnb0"], w=["stg"])
                self.vtt(t1, t1, x_ap, ALU.add, r=["stg", "xch"], w=["stg"])
                self.store(self.xacc_v[:, dc, t0:t0 + 512], t1, r=["stg"])

            self.emit_ln(512, getmo, post_mo)
        es_bc.close()
        self.p.barrier()

    def l0ffn(self):
        self.sc_xint = self.dram("sc_xint", [1024, 2049], F32).ap()
        self.xint_v = self.sc_xint.rearrange("(c p) t -> p c t", p=128)
        for dc in range(8):
            self.bufs[f"xi{dc}"] = Buf(f"xi{dc}")
        stg, stgb = self.t["stg"], self.t["stgb"]
        self.op("gpsimd", lambda e: e.memset(stg[:, 0:8], 0.0), w=["stg"])
        for dc in range(8):
            self.store(self.xint_v[:, dc, 0:1], stg[:, dc:dc + 1],
                       r=["stg"], w=[f"xi{dc}"], slow=True)

        def getx0(dc, t0):
            xap = self.t["xch"][:, (dc % 2) * 512:(dc % 2) * 512 + 512]
            self.load(xap, self.x_v[:, dc, t0:t0 + 512], w=["xch"])
            return xap, ["xch"]

        for ck in range(4):
            t0 = ck * 512

            def post(dc, m, r, t0=t0):
                xap, rb = getx0(dc, t0)
                t1 = stg[:, 0:512]
                self.vtt(t1, xap, m, ALU.subtract, r=["xch", "mrow"], w=["stg"])
                self.vtt(t1, t1, r, ALU.mult, r=["stg", "vrow"], w=["stg"])
                self.store(self.xint_v[:, dc, 1 + t0:1 + t0 + 512], t1,
                           r=["stg"], w=[f"xi{dc}"])

            self.emit_ln(512, lambda dc, t0=t0: getx0(dc, t0), post)

        for lyr in range(_FFN_LAYERS):
            if "attn" in _FFN_PARTS:
                self._ffn_attn(lyr)
            if "glu" in _FFN_PARTS:
                self._ffn_glu(lyr)
            if lyr == 0:
                self._rebuild_bf_from_xint()
        self.p.barrier()

    def _rebuild_bf_from_xint(self):
        bfb = self.t["bfbuf"]
        for dc in range(8):
            for ck in range(4):
                t0 = ck * 512
                xap = self.t["xch"][:, (ck % 2) * 512:(ck % 2) * 512 + 512]
                self.load(xap, self.xint_v[:, dc, 1 + t0:1 + t0 + 512],
                          r=[f"xi{dc}"], w=["xch"])
                self.vcopy(bfb[:, dc * 2049 + 1 + t0: dc * 2049 + 1 + t0 + 512],
                           xap, r=["xch"], w=["bfbuf"], eng="scalar")

    def _ffn_attn(self, lyr):
        stg, stgb, bfb = self.t["stg"], self.t["stgb"], self.t["bfbuf"]
        es_at = ExitStack()
        for nm, shp, dt_ in [
            ("vring", [128, 8 * 520], BF16), ("kring", [128, 4 * 8 * 128], BF16),
            ("qfm", [128, 4 * 512], BF16), ("aqfm", [128, 4 * 512], BF16),
            ("ptb", [128, 8 * 512], BF16), ("opool", [128, 8 * 512], BF16),
            ("qtm", [128, 512], F32), ("ktm", [128, 512], F32),
            ("tbl1", [128, 512], F32), ("tbl2", [128, 512], F32),
            ("qsrep", [128, 512], F32), ("ksrep", [128, 512], F32),
            ("rgrp", [128, 8], F32), ("rgrp2", [128, 8], F32),
            ("khatf", [128, 2], F32),
            ("kh128", [128, 128], BF16), ("maskA", [128, 128], F32),
            ("maskB", [128, 128], F32), ("wosb", [64, 1024], F32),
            ("wosbb", [64, 1024], BF16), ("xiw", [128, 512], F32),
            ("rtmp", [128, 512], F32), ("rtm2", [128, 512], F32),
            ("rcol", [128, 2], F32),
        ]:
            self.sb(nm, shp, dt_, es=es_at)
        self.load(self.t["maskB"][:], self.inp["ffn_mask"], w=["maskB"])
        tA = self.t["maskA"]
        self.op("gpsimd", lambda e: e.memset(tA[:], 0.0), w=["maskA"])
        self.op("gpsimd",
                (lambda e: e.affine_select(out=tA[:], in_=tA[:],
                 compare_op=ALU.is_ge, fill=-1e30, base=0,
                 pattern=[[-1, 128]], channel_multiplier=1)),
                r=["maskA"], w=["maskA"])
        self.op("gpsimd", lambda e: e.memset(self.t["vring"][:], 1.0), w=["vring"])
        self.op("gpsimd", lambda e: e.memset(self.t["kring"][:], 0.0), w=["kring"])
        self.load(self.t["qsrep"][:],
                  self.inp["qs_row"][lyr:lyr + 1, :].broadcast_to([128, 512]),
                  w=["qsrep"])
        self.vts(self.t["qsrep"][:], self.t["qsrep"][:], 8.0, None, ALU.mult,
                 r=["qsrep"], w=["qsrep"])
        self.load(self.t["ksrep"][:],
                  self.inp["ks_row"][lyr:lyr + 1, :].broadcast_to([128, 512]),
                  w=["ksrep"])
        wqv3 = self.inp["wqkv"][lyr].rearrange("(kc p) m -> p kc m", p=128)
        wov = self.inp["wo_f"][lyr].rearrange("(h r) e -> r h e", r=64)

        for w in range(4):
            wbase = 512 * w
            for part in range(3):
                pms = [self.psum(tt, 512) for tt in range(4)]
                for kc in range(8):
                    wslot = self.t[f"wb{kc % 2}"]
                    wsn = f"wb{kc % 2}"
                    self.load(self.t["wf0"][:, 0:512],
                              wqv3[:, kc, part * 512:part * 512 + 512], w=["wf0"])
                    self.vcopy(wslot[:, 0:512], self.t["wf0"][:, 0:512],
                               r=["wf0"], w=[wsn], eng="scalar")
                    for tt in range(4):
                        t0 = wbase + tt * 128
                        sh = 1 if kc >= 4 else 0
                        lhs = bfb[:, kc * 2049 + 1 + t0 - sh:
                                  kc * 2049 + 1 + t0 - sh + 128]
                        pm, bpm = pms[tt]
                        self.mm(pm, lhs, wslot[:, 0:512], kc == 0, kc == 7,
                                r=[wsn, "bfbuf"], w=[bpm])
                for tt in range(4):
                    pm, bpm = pms[tt]
                    g = 4 * w + tt
                    slot = g % 8
                    if part == 2:
                        ov = self.t["vring"].ap().rearrange(
                            "p (s g c) -> p s g c", g=8, c=65)[:, slot, :, 0:64]
                        iv = pm.rearrange("p (g c) -> p g c", c=64)
                        self.op("scalar", (lambda e, o=ov, i=iv:
                                e.activation(out=o, in_=i, func=AF.Copy)),
                                r=[bpm], w=["vring"])
                    else:
                        tm = self.t["qtm"] if part == 0 else self.t["ktm"]
                        tmn = "qtm" if part == 0 else "ktm"
                        self.act(tm[:], pm, AF.Copy, r=[bpm], w=[tmn])
                        self._rotary(part, w, tt, slot)
            for j in range(4):
                self.op("scalar", (lambda e, o=self.t["aqfm"][:, j * 512:(j + 1) * 512],
                        i=self.t["qfm"][:, j * 512:(j + 1) * 512]:
                        e.activation(out=o, in_=i, func=AF.Abs)),
                        r=["qfm"], w=["aqfm"])
            for h in range(8):
                self._ffn_scores_av(lyr, w, h)
                if lyr == 0 and w == 0 and h == 0:
                    self.dbg("ptb", self.t["ptb"][:], BF16, rb=["ptb"])
                    self.dbg("khatf", self.t["khatf"][:], F32, rb=["khatf"])
                    self.dbg("kh128", self.t["kh128"][:], BF16, rb=["kh128"])
            if lyr == 0 and w == 0:
                self.dbg("qfm", self.t["qfm"][:], BF16, rb=["qfm"])
                self.dbg("aqfm", self.t["aqfm"][:], BF16, rb=["aqfm"])
                self.dbg("kring", self.t["kring"][:], BF16, rb=["kring"])
                self.dbg("vring", self.t["vring"][:], BF16, rb=["vring"])
                self.dbg("opool", self.t["opool"][:], BF16, rb=["opool"])
            for ot in range(8):
                dst = self.t["wosb"].ap()[0:64, 0:1024].rearrange(
                    "r (h e) -> r h e", e=128)
                self.load(dst, wov[:, :, ot * 128:(ot + 1) * 128], w=["wosb"])
                self.vcopy(self.t["wosbb"][0:64, :], self.t["wosb"][0:64, :],
                           r=["wosb"], w=["wosbb"], eng="scalar")
                pm, bpm = self.psum(4, 512)
                for h in range(8):
                    self.mm(pm, self.t["wosbb"][0:64, h * 128:(h + 1) * 128],
                            self.t["opool"][0:64, h * 512:(h + 1) * 512],
                            h == 0, h == 7, r=["wosbb", "opool"], w=[bpm])
                self.load(self.t["xiw"][:],
                          self.xint_v[:, ot, 1 + wbase:1 + wbase + 512],
                          r=[f"xi{ot}"], w=["xiw"])
                self.vtt(self.t["xiw"][:], pm, self.t["xiw"][:], ALU.add,
                         r=[bpm, "xiw"], w=["xiw"])
                self.store(self.xint_v[:, ot, 1 + wbase:1 + wbase + 512],
                           self.t["xiw"][:], r=["xiw"], w=[f"xi{ot}"])
        es_at.close()
        self.p.barrier()

    def _rotary(self, part, w, tt, slot):
        stg, stgb = self.t["stg"], self.t["stgb"]
        tm = self.t["qtm"] if part == 0 else self.t["ktm"]
        tmn = "qtm" if part == 0 else "ktm"
        trow = 128 * (4 * w + tt)
        c_in = self.inp["rot_qc"] if part == 0 else self.inp["rot_kc"]
        s_in = self.inp["rot_qs"] if part == 0 else self.inp["rot_ks"]
        self.load(self.t["tbl1"][:], c_in[trow:trow + 128, :], w=["tbl1"])
        self.load(self.t["tbl2"][:], s_in[trow:trow + 128, :], w=["tbl2"])
        srep = self.t["qsrep"] if part == 0 else self.t["ksrep"]
        srn = "qsrep" if part == 0 else "ksrep"
        rg = self.t["rgrp"]
        for h8 in range(8):
            self.act(stg[:, h8 * 64:(h8 + 1) * 64], tm[:, h8 * 64:(h8 + 1) * 64],
                     AF.Square, accum=rg[:, h8:h8 + 1], r=[tmn], w=["stg", "rgrp"])
        rg2 = self.t["rgrp2"]
        self.vts(rg[:], rg[:], 1e-24, None, ALU.max, r=["rgrp"], w=["rgrp"])
        self.act(rg2[:], rg[:], AF.Ln, r=["rgrp"], w=["rgrp2"])
        self.act(rg2[:], rg2[:], AF.Exp, scale=-0.5, r=["rgrp2"], w=["rgrp2"])
        if part == 0 and w == 0 and tt == 0:
            self.dbg("qtm", tm[:], F32, rb=[tmn])
            self.dbg("ssq", self.t["rgrp2"][:], F32, rb=["rgrp2"])
            self.dbg("rgrp", rg2[:], F32, rb=["rgrp2"])
        for h in range(8):
            self.vstt(self.t["rtmp"][:, h * 64:(h + 1) * 64],
                      tm[:, h * 64:(h + 1) * 64], rg2[:, h:h + 1],
                      srep[:, h * 64:(h + 1) * 64], ALU.mult, ALU.mult,
                      r=[tmn, "rgrp2", srn], w=["rtmp"])
        rt, r2 = self.t["rtmp"], self.t["rtm2"]

        def v3(t, a, b):
            return t.ap()[:, 0:512].rearrange("p (g c) -> p g c", c=64)[:, :, a:b]

        self.vtt(r2[:, 0:512], rt[:, 0:512], self.t["tbl1"][:, 0:512], ALU.mult,
                 r=["rtmp", "tbl1"], w=["rtm2"])
        self.op("vector", (lambda e, o=v3(stg, 0, 32), a=v3(rt, 32, 64),
                b=v3(self.t["tbl2"], 0, 32):
                e.tensor_tensor(out=o, in0=a, in1=b, op=ALU.mult)),
                r=["rtmp", "tbl2"], w=["stg"])
        self.op("vector", (lambda e, o=v3(stg, 32, 64), a=v3(rt, 0, 32),
                b=v3(self.t["tbl2"], 32, 64):
                e.tensor_tensor(out=o, in0=a, in1=b, op=ALU.mult)),
                r=["rtmp", "tbl2"], w=["stg"])
        self.op("vector", (lambda e, o=v3(r2, 0, 32), a=v3(r2, 0, 32),
                b=v3(stg, 0, 32):
                e.tensor_tensor(out=o, in0=a, in1=b, op=ALU.subtract)),
                r=["rtm2", "stg"], w=["rtm2"])
        self.op("vector", (lambda e, o=v3(r2, 32, 64), a=v3(r2, 32, 64),
                b=v3(stg, 32, 64):
                e.tensor_tensor(out=o, in0=a, in1=b, op=ALU.add)),
                r=["rtm2", "stg"], w=["rtm2"])
        if part == 0 and w == 0 and tt == 0:
            self.dbg("rtmp", rt[:, 0:512], F32, rb=["rtmp"])
            self.dbg("rtm2", r2[:, 0:512], F32, rb=["rtm2"])
        self.vcopy(stgb[:, 0:512], r2[:, 0:512], r=["rtm2"], w=["stgb"])
        pmT, b7 = self.psum(7, 1024)
        psT = self.t["psT"]
        for j in range(4):
            self.op("tensor", (lambda e, o=psT[:, j * 128:(j + 1) * 128],
                    i=stgb[:, j * 128:(j + 1) * 128], idn=self.t["id_b"][:]:
                    e.transpose(o, i, idn)), r=["stgb", "id_b"], w=[b7])
        for j in range(4):
            if part == 0:
                dst = self.t["qfm"][:, j * 512 + tt * 128: j * 512 + tt * 128 + 128]
                dn = "qfm"
            else:
                dst = self.t["kring"][:, j * 1024 + slot * 128: j * 1024 + slot * 128 + 128]
                dn = "kring"
            self.act(dst, psT[:, j * 128:(j + 1) * 128], AF.Copy, r=[b7], w=[dn])

    def _ffn_scores_av(self, lyr, w, h):
        bh = 64 * (h % 2)
        ot = h // 2
        kr, qf, aq = self.t["kring"], self.t["qfm"], self.t["aqfm"]
        self.op("vector", (lambda e, o=self.t["khatf"][bh:bh + 64, 0:1],
                i=kr[bh:bh + 64, ot * 1024:(ot + 1) * 1024]:
                e.reduce_max(out=o, in_=i, axis=AX.X, apply_absolute_value=True)),
                r=["kring"], w=["khatf"])
        self.act(self.t["khatf"][bh:bh + 64, 1:2], self.t["khatf"][bh:bh + 64, 0:1],
                 AF.Copy, scale=-1.0, r=["khatf"], w=["khatf"])
        self.vts(self.t["kh128"][bh:bh + 64, :], self.t["ones_f"][bh:bh + 64, :],
                 self.t["khatf"][bh:bh + 64, 1:2], None, ALU.mult,
                 r=["ones_f", "khatf"], w=["kh128"])
        ptb = self.t["ptb"]
        for j in range(8):
            if w == 0 and j < 4:
                continue
            g = 4 * (w - 1) + j
            slot = g % 8
            if j < 4:
                qlo, qhi, dcol, mask, mn = 0, 128 * (j + 1), 128 * j, "maskA", "maskA"
            else:
                jp = j - 4
                qlo, qhi, dcol, mask, mn = 128 * jp, 512, 128 * jp, "maskB", "maskB"
            pm, bpm = self.psum(j % 4, 512)
            pspan = pm[:, 0:qhi - qlo]
            self.mm(pspan, kr[bh:bh + 64, ot * 1024 + slot * 128: ot * 1024 + slot * 128 + 128],
                    qf[bh:bh + 64, ot * 512 + qlo: ot * 512 + qhi], True, False,
                    r=["kring", "qfm"], w=[bpm])
            self.mm(pspan, self.t["kh128"][bh:bh + 64, :],
                    aq[bh:bh + 64, ot * 512 + qlo: ot * 512 + qhi], False, True,
                    r=["kh128", "aqfm"], w=[bpm])
            self.vtt(pm[:, dcol - qlo: dcol - qlo + 128],
                     pm[:, dcol - qlo: dcol - qlo + 128], self.t[mask][:],
                     ALU.add, r=[bpm, mn], w=[bpm])
            self.act(ptb[:, j * 512 + qlo: j * 512 + qhi], pspan, AF.Exp,
                     bias=self.ccol(40.0), r=[bpm, "csts"], w=["ptb"])
        psT = self.t["psT"]
        pmT, b7 = self.psum(7, 1024)
        for qt in range(4):
            pm, bpm = self.psum(4 + qt % 2, 65)
            kts = ([j for j in range(qt, 4)] if w > 0 else []) + \
                  [4 + jp for jp in range(0, qt + 1)]
            for i, j in enumerate(kts):
                g = 4 * (w - 1) + j
                slot = g % 8
                self.mm(pm, ptb[:, j * 512 + qt * 128: j * 512 + qt * 128 + 128],
                        self.t["vring"][:, slot * 520 + h * 65: slot * 520 + h * 65 + 65],
                        i == 0, i == len(kts) - 1, r=["ptb", "vring"], w=[bpm])
            rc = self.t["rcol"]
            self.op("vector", (lambda e, o=rc[:, 0:1], i=pm[:, 64:65]:
                    e.reciprocal(out=o, in_=i)), r=[bpm], w=["rcol"])
            self.act(rc[:, 1:2], rc[:, 0:1], AF.Copy, r=["rcol"], w=["rcol"])
            ob = self.t["stgb"][:, qt * 128:qt * 128 + 64]
            self.vts(ob, pm[:, 0:64], rc[:, 1:2], None, ALU.mult,
                     r=[bpm, "rcol"], w=["stgb"])
            self.op("tensor", (lambda e, o=psT[0:64, qt * 128:qt * 128 + 128],
                    i=ob, idn=self.t["id_b"][:]:
                    e.transpose(o, i, idn)), r=["stgb", "id_b"], w=[b7])
        self.act(self.t["opool"][0:64, h * 512:(h + 1) * 512],
                 psT[0:64, 0:512], AF.Copy, r=[b7], w=["opool"])

    def _ffn_glu(self, lyr):
        stg = self.t["stg"]
        es_g = ExitStack()
        for nm, shp, dt_ in [
            ("hn", [128, 8 * 2048], BF16), ("gg", [128, 22 * 1024], BF16),
            ("b1ac", [128, 22], F32), ("b1gc", [128, 22], F32),
            ("b2c", [128, 8], F32), ("lngc", [128, 8], F32), ("lnbc", [128, 8], F32),
            ("xiw2", [128, 512], F32),
        ]:
            self.sb(nm, shp, dt_, es=es_g)
        self.load(self.t["b1ac"][:], self.inp["b1a"][lyr], w=["b1ac"])
        self.load(self.t["b1gc"][:], self.inp["b1g"][lyr], w=["b1gc"])
        self.load(self.t["b2c"][:], self.inp["b2"][lyr], w=["b2c"])
        self.load(self.t["lngc"][:], self.inp["lng_f"][lyr], w=["lngc"])
        self.load(self.t["lnbc"][:], self.inp["lnb_f"][lyr], w=["lnbc"])
        w1v = self.inp["w1"][lyr].rearrange("(kc p) m -> p kc m", p=128)
        w2main = self.inp["w2"][lyr][0:2688].rearrange("(kc p) e -> p kc e", p=128)
        w2tail = self.inp["w2"][lyr][2688:2730]

        for ck in range(4):
            t0 = ck * 512

            def getsx(dc, t0=t0):
                xap = self.t["xch"][:, (dc % 2) * 512:(dc % 2) * 512 + 512]
                sh = 1 if dc >= 4 else 0
                self.load(xap, self.xint_v[:, dc, 1 + t0 - sh:1 + t0 - sh + 512],
                          r=[f"xi{dc}"], w=["xch"])
                return xap, ["xch"]

            def post(dc, m, r, t0=t0):
                xap, rb = getsx(dc, t0)
                t1 = stg[:, 0:512]
                self.vtt(t1, xap, m, ALU.subtract, r=["xch", "mrow"], w=["stg"])
                self.vtt(t1, t1, r, ALU.mult, r=["stg", "vrow"], w=["stg"])
                dst = self.t["hn"][:, dc * 2048 + t0: dc * 2048 + t0 + 512]
                self.vts(dst, t1, self.t["lngc"][:, dc:dc + 1],
                         self.t["lnbc"][:, dc:dc + 1], ALU.mult, ALU.add,
                         r=["stg", "lngc", "lnbc"], w=["hn"])

            self.emit_ln(512, getsx, post)
        for tc2 in range(2):
            base = tc2 * 1024
            for i in range(22):
                wa = 128 if i < 21 else 42
                wba, bba = self.loadW(w1v, i * 128, wa, 8, 0)
                wbg, bbg = self.loadW(w1v, 2730 + i * 128, wa, 8, 1)
                for sub in range(2):
                    pa, ba = self.psum(sub, 512)
                    pg, bg = self.psum(2 + sub, 512)
                    for kc in range(8):
                        rhs = self.t["hn"][:, kc * 2048 + base + sub * 512:
                                           kc * 2048 + base + sub * 512 + 512]
                        self.mm(pa[0:wa, :], wba[:, kc * wa:(kc + 1) * wa], rhs,
                                kc == 0, kc == 7, r=[bba, "hn"], w=[ba])
                        self.mm(pg[0:wa, :], wbg[:, kc * wa:(kc + 1) * wa], rhs,
                                kc == 0, kc == 7, r=[bbg, "hn"], w=[bg])
                    self.act(stg[0:wa, 0:512], pg[0:wa, :], AF.Gelu,
                             bias=self.t["b1gc"][0:wa, i:i + 1], r=[bg, "b1gc"], w=["stg"])
                    dst = self.t["gg"][0:wa, i * 1024 + sub * 512:
                                       i * 1024 + sub * 512 + 512]
                    self.vstt(dst, pa[0:wa, :], self.t["b1ac"][0:wa, i:i + 1],
                              stg[0:wa, 0:512], ALU.add, ALU.mult,
                              r=[ba, "b1ac", "stg"], w=["gg"])
            for ot in range(8):
                wslot = self.t["wb0"]
                dstf = self.t["wf0"].ap()[:, 0:21 * 128].rearrange(
                    "p (kc c) -> p kc c", c=128)
                self.load(dstf, w2main[:, :, ot * 128:(ot + 1) * 128], w=["wf0"])
                self.load(self.t["wf0"][0:42, 21 * 128:21 * 128 + 128],
                          w2tail[:, ot * 128:(ot + 1) * 128], w=["wf0"])
                self.vcopy(wslot[:, 0:22 * 128], self.t["wf0"][:, 0:22 * 128],
                           r=["wf0"], w=["wb0"], eng="scalar")
                for sub in range(2):
                    pm, bpm = self.psum(4 + sub, 512)
                    for kc in range(21):
                        self.mm(pm, wslot[:, kc * 128:(kc + 1) * 128],
                                self.t["gg"][:, kc * 1024 + sub * 512:
                                             kc * 1024 + sub * 512 + 512],
                                kc == 0, False, r=["wb0", "gg"], w=[bpm])
                    self.mm(pm, wslot[0:42, 21 * 128:21 * 128 + 128],
                            self.t["gg"][0:42, 21 * 1024 + sub * 512:
                                         21 * 1024 + sub * 512 + 512],
                            False, True, r=["wb0", "gg"], w=[bpm])
                    t0 = base + sub * 512
                    self.load(self.t["xiw2"][:], self.xint_v[:, ot, 1 + t0:1 + t0 + 512],
                              r=[f"xi{ot}"], w=["xiw2"])
                    self.vstt(self.t["xiw2"][:], pm, self.t["b2c"][:, ot:ot + 1],
                              self.t["xiw2"][:], ALU.add, ALU.add,
                              r=[bpm, "b2c", "xiw2"], w=["xiw2"])
                    self.store(self.xint_v[:, ot, 1 + t0:1 + t0 + 512],
                               self.t["xiw2"][:], r=["xiw2"], w=[f"xi{ot}"])
        es_g.close()
        self.p.barrier()
    def loadWB(self, src_3d, col0, ncols, nkc, slot):
        wb = self.t[f"wb{slot}"]
        bb = f"wb{slot}"
        dst = wb.ap()[:, 0:nkc * ncols].rearrange("p (kc c) -> p kc c", c=ncols)
        self.load(dst, src_3d[:, 0:nkc, col0:col0 + ncols], w=[bb])
        return wb, bb

    def l1(self):
        stg, stgb, bfb = self.t["stg"], self.t["stgb"], self.t["bfbuf"]
        sb = self.sb
        for dc in range(8):
            self.bufs[f"xac{dc}"] = Buf(f"xac{dc}")

        def getx1(dc, t0, store=False):
            a = self.t["xch"][:, 0:512]
            b = stg[:, 1536:2048]
            self.load(a, self.xacc_v[:, dc, t0:t0 + 512], r=[f"xac{dc}"], w=["xch"])
            self.load(b, self.xint_v[:, dc, 1 + t0:1 + t0 + 512],
                      r=[f"xi{dc}"], w=["stg"])
            o = self.t["xch"][:, 512:1024]
            self.vtt(o, a, b, ALU.add, r=["xch", "stg"], w=["xch"])
            if store:
                self.store(self.xacc_v[:, dc, t0:t0 + 512], o,
                           r=["xch"], w=[f"xac{dc}"])
            return o, ["xch"]

        for ck in range(4):
            t0 = ck * 512

            def post(dc, m, r, t0=t0):
                xap, rb = getx1(dc, t0, store=True)
                t1 = stg[:, 0:512]
                self.vtt(t1, xap, m, ALU.subtract, r=["xch", "mrow"], w=["stg"])
                dst = bfb[:, dc * 2049 + 1 + t0: dc * 2049 + 1 + t0 + 512]
                self.vtt(dst, t1, r, ALU.mult, r=["stg", "vrow"], w=["bfbuf"])

            self.emit_ln(512, lambda dc, t0=t0: getx1(dc, t0), post)
        self.p.barrier()

        self.sc_m = self.dram("sc_m", [2048, 1024], BF16).ap()
        self.m_v = self.sc_m.rearrange("(c p) t -> p c t", p=128)
        self.bufs["mspill"] = Buf("mspill")
        stg, stgb = self.t["stg"], self.t["stgb"]
        es_m = ExitStack()
        for nm in ("g_grh", "g_grl", "g_gih", "g_gil", "g_grnh", "g_grnl"):
            self.sb(nm, [128, 8 * 512], BF16, es=es_m)
        for nm in ("wrh", "wrl", "winh", "winl"):
            self.sb(nm, [128, 1024], BF16, es=es_m)
        for nm in ("urh", "url", "uih", "uil"):
            self.sb(nm, [128, 8 * 512], BF16, es=es_m)
        for nm in ("c_crh", "c_crl", "c_cinh", "c_cinl"):
            self.sb(nm, [128, 1024], BF16, es=es_m)
        wrv = self.inp["wrT"].rearrange("(oc p) m -> p oc m", p=128)
        wiv = self.inp["wiT"].rearrange("(oc p) m -> p oc m", p=128)

        def split_w(srcv, kt, hname, lname, negate):
            wf = self.t["wf0"]
            dst = wf.ap()[:, 0:1024].rearrange("p (kc c) -> p kc c", c=128)
            self.load(dst, srcv[:, 0:8, kt * 128:kt * 128 + 128], w=["wf0"])
            base = wf[:, 0:1024]
            if negate:
                self.vts(stg[:, 0:1024], base, -1.0, None, ALU.mult,
                         r=["wf0"], w=["stg"])
                base = stg[:, 0:1024]
                bsrc = ["stg"]
            else:
                bsrc = ["wf0"]
            h, l = self.t[hname], self.t[lname]
            self.vcopy(h[:, 0:1024], base, r=bsrc, w=[hname], eng="scalar")
            self.vcopy(stg[:, 1024:2048], h[:, 0:1024], r=[hname], w=["stg"],
                       eng="vector")
            self.op("vector", (lambda e, o=l[:, 0:1024], a=base, b=stg[:, 1024:2048]:
                    e.tensor_tensor(out=o, in0=a, in1=b, op=ALU.subtract)),
                    r=bsrc + ["stg"], w=[lname])

        def split_psum(pm, bpm, hi_ap, hn, lo_ap, ln_):
            self.act(stg[:, 0:512], pm, AF.Copy, r=[bpm], w=["stg"])
            self.vcopy(hi_ap, stg[:, 0:512], r=["stg"], w=[hn], eng="scalar")
            self.vcopy(stg[:, 512:1024], hi_ap, r=[hn], w=["stg"], eng="vector")
            self.op("vector", (lambda e, o=lo_ap, a=stg[:, 0:512], b=stg[:, 512:1024]:
                    e.tensor_tensor(out=o, in0=a, in1=b, op=ALU.subtract)),
                    r=["stg"], w=[ln_])

        for mc in range(2):
            for nm, srcn in (("g_grh", "dft_gr_h"), ("g_grl", "dft_gr_l"),
                             ("g_gih", "dft_gi_h"), ("g_gil", "dft_gi_l"),
                             ("g_grnh", "dft_grn_h"), ("g_grnl", "dft_grn_l")):
                v = self.inp[srcn].rearrange("(oc p) m -> p oc m", p=128)
                self.load(self.t[nm].ap().rearrange("p (oc m) -> p oc m", m=512),
                          v[:, :, mc * 512:mc * 512 + 512], w=[nm])
            for kt in range(8):
                split_w(wrv, kt, "wrh", "wrl", False)
                split_w(wiv, kt, "winh", "winl", True)
                pmU, bU = self.psum(0, 512)
                pmV, bV = self.psum(1, 512)
                prodU = [("wrh", "g_grh"), ("wrh", "g_grl"), ("wrl", "g_grh"),
                         ("winh", "g_gih"), ("winh", "g_gil"), ("winl", "g_gih")]
                prodV = [("wrh", "g_gih"), ("wrh", "g_gil"), ("wrl", "g_gih"),
                         ("winh", "g_grnh"), ("winh", "g_grnl"), ("winl", "g_grnh")]
                for pm, bpm, prods in ((pmU, bU, prodU), (pmV, bV, prodV)):
                    n = 0
                    for wn, gn in prods:
                        for oc in range(8):
                            self.mm(pm, self.t[wn][:, oc * 128:(oc + 1) * 128],
                                    self.t[gn][:, oc * 512:(oc + 1) * 512],
                                    n == 0, n == 47, r=[wn, gn], w=[bpm])
                            n += 1
                sl = slice(kt * 512, kt * 512 + 512)
                split_psum(pmU, bU, self.t["urh"][:, sl], "urh",
                           self.t["url"][:, sl], "url")
                split_psum(pmV, bV, self.t["uih"][:, sl], "uih",
                           self.t["uil"][:, sl], "uil")
            crv = self.inp["dft_cr_h"].rearrange("(kc p) m -> p kc m", p=128)
            crlv = self.inp["dft_cr_l"].rearrange("(kc p) m -> p kc m", p=128)
            cinv = self.inp["dft_cin_h"].rearrange("(kc p) m -> p kc m", p=128)
            cinlv = self.inp["dft_cin_l"].rearrange("(kc p) m -> p kc m", p=128)
            for dt in range(8):
                for nm, v in (("c_crh", crv), ("c_crl", crlv),
                              ("c_cinh", cinv), ("c_cinl", cinlv)):
                    self.load(self.t[nm].ap().rearrange("p (kc c) -> p kc c", c=128),
                              v[:, 0:8, dt * 128:dt * 128 + 128], w=[nm])
                pm, bpm = self.psum(4, 512)
                prods = [("c_crh", "urh"), ("c_crh", "url"), ("c_crl", "urh"),
                         ("c_cinh", "uih"), ("c_cinh", "uil"), ("c_cinl", "uih")]
                n = 0
                for cn, un in prods:
                    for kc in range(8):
                        self.mm(pm, self.t[cn][:, kc * 128:(kc + 1) * 128],
                                self.t[un][:, kc * 512:(kc + 1) * 512],
                                n == 0, n == 47, r=[cn, un], w=[bpm])
                        n += 1
                self.act(stg[:, 0:512], pm, AF.Copy, r=[bpm], w=["stg"])
                self.vcopy(stgb[:, 0:512], stg[:, 0:512], r=["stg"], w=["stgb"],
                           eng="scalar")
                self.vcopy(stg[:, 512:1024], stgb[:, 0:512], r=["stgb"], w=["stg"],
                           eng="vector")
                self.op("vector", (lambda e, o=stgb[:, 512:1024], a=stg[:, 0:512],
                        b=stg[:, 512:1024]:
                        e.tensor_tensor(out=o, in0=a, in1=b, op=ALU.subtract)),
                        r=["stg"], w=["stgb"])
                self.store(self.m_v[:, dt, mc * 512:mc * 512 + 512],
                           stgb[:, 0:512], r=["stgb"], w=["mspill"])
                self.store(self.m_v[:, 8 + dt, mc * 512:mc * 512 + 512],
                           stgb[:, 512:1024], r=["stgb"], w=["mspill"])
        es_m.close()
        self.p.barrier()

        self.sb("oall", [128, 8 * 2048], BF16)
        self.sb("btot", [128, 8], F32)
        es_a = ExitStack()
        for nm, shp, dt_ in [
            ("vall", [128, 16 * 520], BF16), ("ptb1", [128, 16 * 512], BF16),
            ("qbuf", [128, 2048], BF16), ("kbuf", [128, 2048], BF16),
            ("kaugs", [4, 2048], BF16),
            ("qa0s", [4, 2048], F32), ("qaugP", [4, 2048], BF16),
            ("qaugM", [4, 2048], BF16), ("dts", [128, 128], F32),
            ("ostg", [64, 512], BF16), ("rc1", [128, 2], F32),
            ("wvb", [128, 8 * 512], BF16),
        ]:
            self.sb(nm, shp, dt_, es=es_a)
        for q in range(8):
            self.bufs[f"qsp{q}"] = Buf(f"qsp{q}")
        for q in range(4):
            self.bufs[f"ksp{q}"] = Buf(f"ksp{q}")
        self.load(self.t["kaugs"][:], self.inp["kaug"], w=["kaugs"])
        self.load(self.t["qa0s"][:], self.inp["qa0"], w=["qa0s"])
        self.load(self.t["dts"][:], self.inp["dtile"], w=["dts"])
        self.load(self.t["btot"][:], self.inp["bo"], w=["btot"])
        self.load(stg[:, 0:8], self.inp["l1fb"], w=["stg"])
        self.op("gpsimd", (lambda e, o=self.t["btot"][:], a=self.t["btot"][:],
                b=stg[:, 0:8]: e.tensor_tensor(out=o, in0=a, in1=b, op=ALU.add)),
                r=["btot", "stg"], w=["btot"])
        self.op("gpsimd", lambda e: e.memset(self.t["vall"][:], 1.0), w=["vall"])
        wqv = self.inp["wqT"].rearrange("(kc p) m -> p kc m", p=128)
        wkv = self.inp["wkT"].rearrange("(kc p) m -> p kc m", p=128)
        for ot in range(8):
            wb, bb = self.loadW(wqv, ot * 128, 128, 8, ot % 2)
            for tc4 in range(4):
                pm, bpm = self.psum(tc4 % 2, 512)
                for kc in range(8):
                    self.mm(pm, wb[:, kc * 128:(kc + 1) * 128],
                            bfb[:, kc * 2049 + 1 + tc4 * 512: kc * 2049 + 1 + tc4 * 512 + 512],
                            kc == 0, kc == 7, r=[bb, "bfbuf"], w=[bpm])
                self.act(stgb[:, tc4 * 512:tc4 * 512 + 512], pm, AF.Copy,
                         r=[bpm], w=["stgb"])
            self.store(self.u_v[:, ot, :], stgb[:, 0:2048], r=["stgb"],
                       w=[f"qsp{ot}"])
        for ot in range(4):
            wb, bb = self.loadW(wkv, ot * 128, 128, 8, ot % 2)
            for tc4 in range(4):
                pm, bpm = self.psum(tc4 % 2, 512)
                for kc in range(8):
                    self.mm(pm, wb[:, kc * 128:(kc + 1) * 128],
                            bfb[:, kc * 2049 + 1 + tc4 * 512: kc * 2049 + 1 + tc4 * 512 + 512],
                            kc == 0, kc == 7, r=[bb, "bfbuf"], w=[bpm])
                self.act(stgb[:, tc4 * 512:tc4 * 512 + 512], pm, AF.Copy,
                         r=[bpm], w=["stgb"])
            self.store(self.zs_v[:, ot, :], stgb[:, 0:2048], r=["stgb"],
                       w=[f"ksp{ot}"])
        wvv = self.inp["wvT"].rearrange("(kc p) m -> p kc m", p=128)
        for g in range(2):
            dstf = self.t["wf0"].ap()[:, 0:2048].rearrange("p (kc c) -> p kc c", c=512)
            self.load(dstf, wvv[:, g * 4:(g + 1) * 4, 0:512], w=["wf0"])
            self.vcopy(self.t["wvb"][:, g * 2048:(g + 1) * 2048],
                       self.t["wf0"][:, 0:2048], r=["wf0"], w=["wvb"], eng="scalar")
        for tt in range(16):
            pm, bpm = self.psum(tt % 2, 512)
            for kc in range(8):
                self.mm(pm, bfb[:, kc * 2049 + 1 + tt * 128: kc * 2049 + 1 + tt * 128 + 128],
                        self.t["wvb"][:, kc * 512:(kc + 1) * 512],
                        kc == 0, kc == 7, r=["wvb", "bfbuf"], w=[bpm])
            ov = self.t["vall"].ap().rearrange("p (s g c) -> p s g c", g=8, c=65)[:, tt, :, 0:64]
            iv = pm.rearrange("p (g c) -> p g c", c=64)
            self.op("scalar", (lambda e, o=ov, i=iv:
                    e.activation(out=o, in_=i, func=AF.Copy)), r=[bpm], w=["vall"])
        psT = self.t["psT"]
        for h in range(16):
            bh = 64 * (h % 2)
            kv = h // 2
            slope = float(2.0 ** (-(h + 1)))
            self.load(self.t["qbuf"][bh:bh + 64, :],
                      self.u_v[64 * (h % 2):64 * (h % 2) + 64, h // 2, :],
                      r=[f"qsp{h // 2}"], w=["qbuf"])
            self.load(self.t["kbuf"][bh:bh + 64, :],
                      self.zs_v[64 * (kv % 2):64 * (kv % 2) + 64, kv // 2, :],
                      r=[f"ksp{kv // 2}"], w=["kbuf"])
            self.act(self.t["qaugP"][0:4, :], self.t["qa0s"][0:4, :], AF.Copy,
                     scale=-slope, r=["qa0s"], w=["qaugP"])
            self.act(self.t["qaugM"][0:4, :], self.t["qa0s"][0:4, :], AF.Copy,
                     scale=slope, r=["qa0s"], w=["qaugM"])
            for qb in range(4):
                for kt in range(16):
                    pm, bpm = self.psum(kt % 4, 512)
                    rel = kt - 4 * qb
                    if rel < 0 or rel > 3:
                        va = "qaugP" if rel < 0 else "qaugM"
                        self.mm(pm, self.t["kbuf"][bh:bh + 64, kt * 128:(kt + 1) * 128],
                                self.t["qbuf"][bh:bh + 64, qb * 512:(qb + 1) * 512],
                                True, False, r=["kbuf", "qbuf"], w=[bpm])
                        self.mm(pm, self.t["kaugs"][0:4, kt * 128:(kt + 1) * 128],
                                self.t[va][0:4, qb * 512:(qb + 1) * 512],
                                False, True, r=["kaugs", va], w=[bpm])
                    else:
                        first = True
                        for sub in range(4):
                            q0 = qb * 512 + sub * 128
                            pspan = pm[:, sub * 128:(sub + 1) * 128]
                            last = sub == 3
                            if sub == rel:
                                self.mm(pspan,
                                        self.t["kbuf"][bh:bh + 64, kt * 128:(kt + 1) * 128],
                                        self.t["qbuf"][bh:bh + 64, q0:q0 + 128],
                                        first, last, r=["kbuf", "qbuf"], w=[bpm])
                                self.vstt(pspan, self.t["dts"][:], slope, pspan,
                                          ALU.mult, ALU.add, r=["dts", bpm], w=[bpm])
                            else:
                                va = "qaugP" if sub > rel else "qaugM"
                                self.mm(pspan,
                                        self.t["kbuf"][bh:bh + 64, kt * 128:(kt + 1) * 128],
                                        self.t["qbuf"][bh:bh + 64, q0:q0 + 128],
                                        first, False, r=["kbuf", "qbuf"], w=[bpm])
                                self.mm(pspan,
                                        self.t["kaugs"][0:4, kt * 128:(kt + 1) * 128],
                                        self.t[va][0:4, q0:q0 + 128],
                                        False, last, r=["kaugs", va], w=[bpm])
                            first = False
                    if kt % 4 == 3:
                        pall = self.t["psA"][:, 0:2048]
                        bps = [f"PS{i}" for i in range(4)]
                        self.op("scalar", (lambda e,
                                o=self.t["ptb1"][:, (kt - 3) * 512:(kt + 1) * 512],
                                i=pall: e.activation(out=o, in_=i, func=AF.Exp,
                                                     scale=0.125)),
                                r=bps, w=["ptb1"])
                for qt in range(4):
                    pm, bpm = self.psum(4 + qt % 2, 65)
                    for kt in range(16):
                        self.mm(pm, self.t["ptb1"][:, kt * 512 + qt * 128: kt * 512 + qt * 128 + 128],
                                self.t["vall"][:, kt * 520 + kv * 65: kt * 520 + kv * 65 + 65],
                                kt == 0, kt == 15, r=["ptb1", "vall"], w=[bpm])
                    rc = self.t["rc1"]
                    self.op("vector", (lambda e, o=rc[:, 0:1], i=pm[:, 64:65]:
                            e.reciprocal(out=o, in_=i)), r=[bpm], w=["rc1"])
                    self.act(rc[:, 1:2], rc[:, 0:1], AF.Copy, r=["rc1"], w=["rc1"])
                    ob = stgb[:, qt * 128:qt * 128 + 64]
                    self.vts(ob, pm[:, 0:64], rc[:, 1:2], None, ALU.mult,
                             r=[bpm, "rc1"], w=["stgb"])
                    pmT, b7 = self.psum(7, 1024)
                    self.op("tensor", (lambda e, o=psT[0:64, qt * 128:qt * 128 + 128],
                            i=ob, idn=self.t["id_b"][:]:
                            e.transpose(o, i, idn)), r=["stgb", "id_b"], w=[b7])
                pmT, b7 = self.psum(7, 1024)
                self.act(self.t["ostg"][0:64, 0:512], psT[0:64, 0:512], AF.Copy,
                         r=[b7], w=["ostg"])
                self.store(self.t["oall"][bh:bh + 64,
                           (h // 2) * 2048 + qb * 512:(h // 2) * 2048 + qb * 512 + 512],
                           self.t["ostg"][0:64, 0:512], r=["ostg"], w=["oall"])
        self.dbg("oall", self.t["oall"][:], BF16, rb=["oall"])
        es_a.close()
        self.p.barrier()
        stg, stgb, bfb = self.t["stg"], self.t["stgb"], self.t["bfbuf"]
        es_f = ExitStack()
        self.sb("mbh", [128, 8 * 1024], BF16, es=es_f)
        self.sb("mbl", [128, 8 * 1024], BF16, es=es_f)
        self.sb("xn1lo", [128, 8 * 512], BF16, es=es_f)
        for dt in range(8):
            self.load(self.t["mbh"][:, dt * 1024:(dt + 1) * 1024],
                      self.m_v[:, dt, :], r=["mspill"], w=["mbh"])
            self.load(self.t["mbl"][:, dt * 1024:(dt + 1) * 1024],
                      self.m_v[:, 8 + dt, :], r=["mspill"], w=["mbl"])
        self.dbg("mbh", self.t["mbh"][:], BF16, rb=["mbh"])
        self.dbg("mbl", self.t["mbl"][:], BF16, rb=["mbl"])
        wov = self.inp["woT"].rearrange("(kc p) m -> p kc m", p=128)
        for ck in range(4):
            t0 = ck * 512

            def getx1b(dc, t0=t0):
                xap = self.t["xch"][:, (dc % 2) * 512:(dc % 2) * 512 + 512]
                self.load(xap, self.xacc_v[:, dc, t0:t0 + 512],
                          r=[f"xac{dc}"], w=["xch"])
                return xap, ["xch"]

            def post_lo(dc, m, r, t0=t0):
                xap, rb = getx1b(dc, t0)
                t1 = stg[:, 0:512]
                self.vtt(t1, xap, m, ALU.subtract, r=["xch", "mrow"], w=["stg"])
                self.vtt(t1, t1, r, ALU.mult, r=["stg", "vrow"], w=["stg"])
                hi = bfb[:, dc * 2049 + 1 + t0: dc * 2049 + 1 + t0 + 512]
                self.vcopy(stg[:, 512:1024], hi, r=["bfbuf"], w=["stg"], eng="vector")
                self.op("vector", (lambda e,
                        o=self.t["xn1lo"][:, dc * 512:(dc + 1) * 512],
                        a=t1, b=stg[:, 512:1024]:
                        e.tensor_tensor(out=o, in0=a, in1=b, op=ALU.subtract)),
                        r=["stg"], w=["xn1lo"])

            self.emit_ln(512, getx1b, post_lo)
            for ot in range(8):
                wb, bb = self.loadW(wov, ot * 128, 128, 8, ot % 2)
                pm, bpm = self.psum(4 + ot % 2, 512)
                for oc in range(8):
                    self.mm(pm, wb[:, oc * 128:(oc + 1) * 128],
                            self.t["oall"][:, oc * 2048 + t0: oc * 2048 + t0 + 512],
                            oc == 0, False, r=[bb, "oall"], w=[bpm])
                for dc in range(8):
                    hi = bfb[:, dc * 2049 + 1 + t0: dc * 2049 + 1 + t0 + 512]
                    self.mm(pm, self.t["mbh"][:, dc * 1024 + ot * 128: dc * 1024 + ot * 128 + 128],
                            hi, False, False, r=["mbh", "bfbuf"], w=[bpm])
                    self.mm(pm, self.t["mbl"][:, dc * 1024 + ot * 128: dc * 1024 + ot * 128 + 128],
                            hi, False, False, r=["mbl", "bfbuf"], w=[bpm])
                    self.mm(pm, self.t["mbh"][:, dc * 1024 + ot * 128: dc * 1024 + ot * 128 + 128],
                            self.t["xn1lo"][:, dc * 512:(dc + 1) * 512],
                            False, dc == 7, r=["mbh", "xn1lo"], w=[bpm])
                xw = self.t["xch"][:, 0:512]
                self.load(xw, self.xacc_v[:, ot, t0:t0 + 512],
                          r=[f"xac{ot}"], w=["xch"])
                t1 = stg[:, 0:512]
                self.vstt(t1, pm, self.t["btot"][:, ot:ot + 1], xw,
                          ALU.add, ALU.add, r=[bpm, "btot", "xch"], w=["stg"])
                st = self.store(self.y_v[:, ot, t0:t0 + 512], t1, r=["stg"])
                self.out_store_ops.append(st)
        es_f.close()

    def finalize_debug_copy(self, add_xint=False):
        for dc in range(8):
            for ck in range(4):
                t0 = ck * 512
                a = self.t["stg"][:, 0:512]
                self.load(a, self.xacc_v[:, dc, t0:t0 + 512],
                          r=[f"xac{dc}" if f"xac{dc}" in self.bufs else "stg"], w=["stg"])
                if add_xint:
                    b = self.t["stg"][:, 512:1024]
                    self.load(b, self.xint_v[:, dc, 1 + t0:1 + t0 + 512],
                              r=[f"xi{dc}"], w=["stg"])
                    self.vtt(a, a, b, ALU.add, r=["stg"], w=["stg"])
                st = self.store(self.y_v[:, dc, t0:t0 + 512], a, r=["stg"])
                self.out_store_ops.append(st)


def build_program(stages=("mamba", "ffn", "l1")):
    nc = bass.Bass("TRN2", target_bir_lowering=False)
    with ExitStack() as es:
        mk = MK(nc, es)
        mk.declare_io()
        mk.alloc_persistent()
        sems = {}
        for e in list(ENGINES) + ["gdma"]:
            sems[e] = es.enter_context(nc.semaphore("sem_" + e))
        mk.init_consts()
        if "mamba" in stages:
            mk.mamba()
        if "ffn" in stages:
            mk.l0ffn()
        if "l1" in stages:
            mk.l1()
        else:
            mk.finalize_debug_copy(add_xint=("ffn" in stages))
        mk.p.emit_all(nc, es.enter_context(nc.Block()), sems,
                      final_waits=mk.out_store_ops)
    return nc


_STAGES = ("mamba", "ffn", "l1")
_FFN_LAYERS = 2
_FFN_PARTS = ("attn", "glu")
_DBG = ()


def kernel(**inputs):
    in_maps = _prep_host(inputs)
    maps = []
    for m in in_maps:
        mm_ = {}
        for k, v in m.items():
            mm_["d_" + k] = v
        maps.append(mm_)
    nc = build_program(_STAGES)
    res = run_bass_kernel_spmd(nc, maps, core_ids=list(range(NCORES)))
    out = np.stack([np.ascontiguousarray(r["y"].T) for r in res.results], axis=0)
    return out.astype(np.float32)
